# version 05 (final)
# baseline (speedup 1.0000x reference)
import contextlib
import numpy as np
import concourse.bass as bass
import concourse.mybir as mybir

F32 = mybir.dt.float32
BF16 = mybir.dt.bfloat16
AF = mybir.ActivationFunctionType
ALU = mybir.AluOpType
AX = mybir.AxisListType

B = 8
NLOC = 512
D = 1024
R = B * NLOC
KC = D // 128
NCORES = 8
EPS_LN = 1e-5
DT_B = 0.1
EPS_PERF = 1e-4
HEADS = 16
HDIM = 64
NFEAT = 64
S_PERF = float(HDIM) ** (-0.25)

ENGINES = ("sync", "scalar", "vector", "tensor", "gpsimd")

ARENA = 47104
DEBUG_CDUMP = None
SERIAL_WAVES = False
N_STEPS = B
SKIP_H = False
NO_KVPS_BANKS = False
DEBUG_XDUMP = None
NO_COLLECTIVE = False
ATTN_STAGE = 5
ATTN_POST = 5
PROJ_REV = False
DEBUG_PROJ = None


class Sem:
    __slots__ = ("h", "n", "name")

    def __init__(self, h, name):
        self.h = h
        self.n = 0
        self.name = name


class Prog:
    def __init__(self):
        self.ops = {e: [] for e in ENGINES}
        self.dma_issued = {}

    def add(self, eng, fn):
        self.ops[eng].append(fn)

    def wait(self, eng, sem, val):
        if val > 0:
            self.add(eng, lambda e, h=sem.h, v=val: e.wait_ge(h, v))

    def run(self, eng, fn, inc=None, inc_k=1):
        if inc is None:
            self.add(eng, fn)
            return None
        inc.n += inc_k
        def g(e, fn=fn, h=inc.h, k=inc_k):
            fn(e).then_inc(h, k)
        self.add(eng, g)
        return inc.n

    def dma(self, eng, out, in_, inc, slow=False):
        self.dma_issued[inc] = eng
        if slow:
            return self.run(eng, lambda e, o=out, i=in_: e.dma_start(
                out=o, in_=i, allow_slow_non_contiguous=True), inc=inc, inc_k=16)
        return self.run(eng, lambda e, o=out, i=in_: e.dma_start(out=o, in_=i),
                        inc=inc, inc_k=16)

    def replay(self, nc):
        with nc.Block() as block:
            for eng in ENGINES:
                ops = self.ops[eng]
                if not ops:
                    continue
                def body(e, ops=ops):
                    for f in ops:
                        f(e)
                getattr(block, eng)(body)


class Ring:

    def __init__(self, K, tag, n):
        self.n = n
        self.sems = [K.sem(f"{tag}{i}") for i in range(n)]
        self.free_mark = [None] * n
        self.i = 0

    def acquire(self):
        s = self.i % self.n
        self.i += 1
        return s, self.free_mark[s], self.sems[s]

    def release(self, slot, sem, count):
        self.free_mark[slot] = (sem, count)


def bcastf(ap, reps, after=0):
    dims = [list(d) for d in ap.ap]
    dims.insert(1 + after, [0, reps])
    return bass.AP(ap.tensor, ap.offset, dims)


def mmfn(out, lhsT, rhs, start, stop):
    return lambda e, o=out, l=lhsT, r=rhs, s0=start, s1=stop: e.matmul(
        o, l, r, start=s0, stop=s1)


def ttfn(out, a, b_, op):
    return lambda e, o=out, x=a, y=b_, p=op: e.tensor_tensor(o, x, y, p)


def tsfn(out, a, s1, op0, s2=None, op1=None, accum=None):
    return lambda e, o=out, x=a, s=s1, p=op0, s_=s2, p_=op1, ac=accum: \
        e.tensor_scalar(o, x, s, s_, p, *(() if p_ is None else (p_,)),
                        **({} if ac is None else {"accum_out": ac}))


def sttfn(out, in0, scalar, in1, op0, op1, accum=None):
    return lambda e, o=out, a=in0, s=scalar, b_=in1, p0=op0, p1=op1, ac=accum: \
        e.scalar_tensor_tensor(o, a, s, b_, p0, p1,
                               **({} if ac is None else {"accum_out": ac}))


def actfn(out, in_, func, bias=0.0, scale=1.0, accum=None):
    return lambda e, o=out, i=in_, f=func, b_=bias, s=scale, ac=accum: e.activation(
        o, i, f, bias=b_, scale=s, **({} if ac is None else {"accum_out": ac}))


def cpfn(out, in_):
    return lambda e, o=out, i=in_: e.tensor_copy(o, i)


def redfn(out, in_, op, negate=False):
    return lambda e, o=out, i=in_, p=op, ng=negate: e.tensor_reduce(
        o, i, AX.X, p, negate=ng)


def scanfn(out, d0, d1, init, op0, op1):
    return lambda e, o=out, a=d0, b_=d1, i=init, p0=op0, p1=op1: \
        e.tensor_tensor_scan(o, a, b_, i, p0, p1)


def memsetfn(ap, v=0.0):
    return lambda e, o=ap, x=v: e.memset(o, x)


class Kern:
    def __init__(self, nc):
        self.nc = nc
        self.p = Prog()
        self.sems = {}
        self._ctx = contextlib.ExitStack()
        self.pb = [nc.alloc_psum_tensor(f"pb{i}", [128, 512], F32) for i in range(6)]
        self.kvps = nc.alloc_psum_tensor("kvps", [128, 1024], F32)
        g = nc.alloc_sbuf_tensor
        self.m_stage = g("g_mstage", [1, 512], F32)
        self.r_stage = g("g_rstage", [1, 512], F32)
        self.m_in = g("g_min", [1, 2 * 512], F32)
        self.r_in = g("g_rin", [1, 2 * 512], F32)
        self.stats_hbm = None
        self.stats_stored = {}
        self.scr_a = g("g_scra", [1, 512], F32)
        self.scr_b = g("g_scrb", [1, 512], F32)
        self.xstage = g("g_xstage", [128, 3 * 512], F32)
        self.xsq = g("g_xsq", [128, 512], F32)
        self.tstage = g("g_tstage", [128, 512], F32)
        self.tstage2 = g("g_tstage2", [128, 512], F32)
        self.ones_col_f32 = g("g_ones_col", [128, 1], F32)
        self.ones_row_f32 = g("g_ones_row", [1, 128], F32)
        self.ones_row_bf = g("g_ones_row_bf", [1, 128], BF16)
        self.ones_col_bf = g("g_ones_col_bf", [128, 1], BF16)
        self.ident_bf = g("g_ident", [128, 128], BF16)
        self.cpool = g("g_cpool", [128, 16], F32)
        self.cpool_vals = []
        self.xrring = Ring(self, "xr", 3)
        self.srring = Ring(self, "sr", 2)
        self.const_loads = []

    def cidx(self, value):
        if value not in self.cpool_vals:
            assert len(self.cpool_vals) < 16
            self.cpool_vals.append(value)
        return self.cpool_vals.index(value)

    def cv(self, value, prows=128):
        if value not in self.cpool_vals:
            assert len(self.cpool_vals) < 16
            self.cpool_vals.append(value)
        i = self.cpool_vals.index(value)
        return self.cpool[0:prows, i:i + 1]

    def sem(self, name):
        if name not in self.sems:
            h = self._ctx.enter_context(self.nc.semaphore(name))
            self.sems[name] = Sem(h, name)
        return self.sems[name]

    def retire_all_dma(self):
        for sem, eng in self.p.dma_issued.items():
            self.p.wait(eng, sem, sem.n)

    def barrier(self, tag):
        p = self.p
        bs = self.sem("bar")
        base = bs.n
        for eng in ENGINES:
            bs.n += 1
            p.add(eng, lambda e, h=bs.h: e.sem_inc(h, 1))
        tgt = base + len(ENGINES)
        for eng in ENGINES:
            p.wait(eng, bs, tgt)

    def hard_boundary(self):
        for s_ in list(self.sems.values()):
            if s_.n <= 0:
                continue
            for eng in ENGINES:
                self.p.wait(eng, s_, s_.n)

    def load_consts(self, cmap, sem):
        for o, i in cmap:
            self.p.dma("gpsimd", o, i, sem)



def emit_ln_stats(K, x_dram, tag):
    p = K.p
    sst = K.sem("st_st")
    ssq = K.sem("st_sq")
    spe = K.sem("st_pe")
    sve = K.sem("st_ve")
    sse = K.sem("st_se")
    spe_b = spe.n
    for rt in range(B):
        for k in range(KC):
            i = rt * KC + k
            slot, freed, sl = K.xrring.acquire()
            xs = K.xstage[:, slot * 512:(slot + 1) * 512]
            if freed is not None:
                p.wait("sync", *freed)
            p.dma("sync", xs, x_dram[k * 128:(k + 1) * 128,
                                     rt * 512:(rt + 1) * 512], sl)
            sq = K.xsq[:, 0:512]
            p.wait("scalar", sl, sl.n)
            if i >= 1:
                p.wait("scalar", spe, spe_b + 2 * i)
            p.run("scalar", actfn(sq, xs, AF.Square), inc=ssq)
            if k == 0:
                p.wait("tensor", sve, sve.n)
            p.wait("tensor", sl, sl.n)
            p.run("tensor", mmfn(K.pb[4][0:1, :], K.ones_col_f32[:, :], xs,
                                 k == 0, k == KC - 1), inc=spe)
            p.wait("tensor", ssq, ssq.n)
            p.run("tensor", mmfn(K.pb[5][0:1, :], K.ones_col_f32[:, :], sq,
                                 k == 0, k == KC - 1), inc=spe)
            K.xrring.release(slot, spe, spe.n)
        c0, c1 = rt * 512, (rt + 1) * 512
        mean = K.m_stage[0:1, :]
        rstd = K.r_stage[0:1, :]
        p.wait("vector", spe, spe.n)
        if rt >= 1:
            p.wait("vector", sst, sst.n)
        p.run("vector", tsfn(mean, K.pb[4][0:1, :], 1.0 / D, ALU.mult))
        p.run("vector", tsfn(K.scr_b[0:1, :], K.pb[5][0:1, :], 1.0 / D, ALU.mult),
              inc=sve)
        p.wait("vector", sve, sve.n)
        p.run("vector", ttfn(K.scr_a[0:1, :], mean, mean, ALU.mult), inc=sve)
        p.wait("vector", sve, sve.n)
        p.run("vector", ttfn(K.scr_b[0:1, :], K.scr_b[0:1, :], K.scr_a[0:1, :],
                             ALU.subtract), inc=sve)
        p.wait("scalar", sve, sve.n)
        p.run("scalar", actfn(K.scr_a[0:1, :], K.scr_b[0:1, :], AF.Sqrt,
                              bias=K.cv(EPS_LN, 1)), inc=sse)
        p.wait("vector", sse, sse.n)
        p.run("vector", lambda e, o=rstd, i=K.scr_a[0:1, :]: e.reciprocal(o, i),
              inc=sve)
        p.wait("vector", sve, sve.n)
        p.run("vector", tsfn(K.scr_b[0:1, :], K.scr_b[0:1, :], EPS_LN, ALU.add),
              inc=sve)
        p.wait("vector", sve, sve.n)
        p.run("vector", ttfn(K.scr_a[0:1, :], rstd, rstd, ALU.mult), inc=sve)
        p.wait("vector", sve, sve.n)
        p.run("vector", ttfn(K.scr_a[0:1, :], K.scr_a[0:1, :], K.scr_b[0:1, :],
                             ALU.mult), inc=sve)
        p.wait("vector", sve, sve.n)
        h15 = bass.AP(K.cpool.ap().tensor, K.cidx(1.5), [[16, 1], [0, 512]])
        p.run("vector", sttfn(K.scr_a[0:1, :], K.scr_a[0:1, :], -0.5, h15,
                              ALU.mult, ALU.add), inc=sve)
        p.wait("vector", sve, sve.n)
        p.run("vector", ttfn(rstd, rstd, K.scr_a[0:1, :], ALU.mult), inc=sve)
        p.wait("sync", sve, sve.n)
        p.dma("sync", K.stats_hbm[0:1, c0:c1], mean, sst)
        p.dma("sync", K.stats_hbm[1:2, c0:c1], rstd, sst)
        K.stats_stored[rt] = (sst, sst.n)
    return sve, sve.n



def emit_xln_batch(K, x_dram, bt, dst_of_k, tag, tensor_wait=None, banks=(4, 5)):
    p = K.p
    spe = K.sem("xl_pe")
    sve = K.sem("xl_ve")
    c0, c1 = bt * 512, (bt + 1) * 512
    slot, freed, srl = K.srring.acquire()
    if freed is not None:
        p.wait("sync", *freed)
    p.wait("sync", *K.stats_stored[bt])
    m_in = K.m_in[0:1, slot * 512:(slot + 1) * 512]
    r_in = K.r_in[0:1, slot * 512:(slot + 1) * 512]
    p.dma("sync", m_in, K.stats_hbm[0:1, c0:c1], srl)
    p.dma("sync", r_in, K.stats_hbm[1:2, c0:c1], srl)
    if tensor_wait is not None:
        p.wait("tensor", *tensor_wait)
    p.wait("tensor", sve, sve.n)
    p.wait("tensor", srl, srl.n)
    mrep, rrep = K.pb[banks[0]], K.pb[banks[1]]
    p.run("tensor", mmfn(mrep[:, :], K.ones_row_f32[:, :], m_in, True, True),
          inc=spe)
    p.run("tensor", mmfn(rrep[:, :], K.ones_row_f32[:, :], r_in, True, True),
          inc=spe)
    K.srring.release(slot, spe, spe.n)
    for k2 in range(0, KC, 2):
        xss = []
        for k in (k2, k2 + 1):
            slot, freed, sl = K.xrring.acquire()
            xs = K.xstage[:, slot * 512:(slot + 1) * 512]
            if freed is not None:
                p.wait("sync", *freed)
            p.dma("sync", xs, x_dram[k * 128:(k + 1) * 128, c0:c1], sl)
            xss.append((slot, xs, sl))
        p.wait("vector", xss[0][2], xss[0][2].n)
        p.wait("vector", xss[1][2], xss[1][2].n)
        p.wait("vector", spe, spe.n)
        p.run("vector", ttfn(K.tstage[:, 0:512], xss[0][1], mrep[:, :],
                             ALU.subtract))
        p.run("vector", ttfn(K.tstage2[:, 0:512], xss[1][1], mrep[:, :],
                             ALU.subtract))
        p.run("vector", ttfn(dst_of_k(k2), K.tstage[:, 0:512], rrep[:, :],
                             ALU.mult), inc=sve)
        p.run("vector", ttfn(dst_of_k(k2 + 1), K.tstage2[:, 0:512], rrep[:, :],
                             ALU.mult), inc=sve)
        K.xrring.release(xss[0][0], sve, sve.n)
        K.xrring.release(xss[1][0], sve, sve.n)
    return sve, sve.n



def emit_ffn(K, x_in, x_out, wd, bd, inc_dram, xln_hbm, tag):
    p, nc = K.p, K.nc
    at = nc.alloc_sbuf_tensor_at
    st = lambda s: K.sem("fn" + s)
    swl, spe, sse, sve = st("_wl"), st("_pe"), st("_se"), st("_ve")
    ssp = [st(f"_sp{i}") for i in range(3)]
    shl = st("_hl")
    shv = st("_hv")
    spe_base, sse_base, sve_base, shv_base = spe.n, sse.n, sve.n, shv.n
    off = ARENA
    def A(name, shape, dt):
        nonlocal off
        t = at(tag + name, shape, dt, offset=off)
        nbytes = shape[1] * (2 if dt == BF16 else 4)
        off += (nbytes + 31) & ~31
        return t
    gwbase = off
    gw = {nm: A(nm, [128, KC * 1024], BF16)
          for nm in ("wfT", "wiT", "wcT", "ufT", "uiT", "ucT")}
    xln = A("xln", [128, 3 * KC * 512], BF16)
    cbf = [A(f"c{i}", [128, KC * 512], F32) for i in (0, 1)]
    cmv = A("cmv", [128, KC * 512], BF16)
    fig = [A(f"fig{i}", [128, 3 * 512], F32) for i in (0, 1)]
    cscr = A("cscr", [128, 2 * 512], F32)
    bcols = A("bcols", [128, 4 * KC], F32)
    inc_col = A("incc", [128, KC], F32)
    assert off <= 229344, off

    woT = at(tag + "woT", [128, KC * 1024], BF16, offset=gwbase)
    uoT = at(tag + "uoT", [128, KC * 1024], BF16, offset=gwbase + KC * 2048)
    hsb = [at(tag + f"h{i}", [128, 512], F32,
              offset=gwbase + 2 * KC * 2048 + i * 2048) for i in (0, 1)]
    xres = at(tag + "xres", [128, 4 * 512], F32, offset=gwbase + 2 * KC * 2048 + 4096)
    xost = at(tag + "xost", [128, 4 * 512], F32,
              offset=gwbase + 2 * KC * 2048 + 4096 + 8192)

    for nm in ("wfT", "wiT", "wcT", "ufT", "uiT", "ucT"):
        for k in range(KC):
            p.dma("gpsimd", gw[nm][:, k * 1024:(k + 1) * 1024],
                  wd[nm][k * 128:(k + 1) * 128, :], swl)
    for i, nm in enumerate(("bf", "bi", "bc", "bo")):
        ap = bass.AP(bd[nm], 0, [[1, 128], [128, KC]])
        p.dma("gpsimd", bcols[:, i * KC:(i + 1) * KC], ap, swl, slow=True)
    ap = bass.AP(inc_dram, 0, [[1, 128], [128, KC]])
    p.dma("gpsimd", inc_col[:, :], ap, swl, slow=True)
    wl_total = swl.n

    stats_sem, stats_n = emit_ln_stats(K, x_in, tag + "s")

    p.wait("vector", swl, wl_total)
    zsrc = bass.AP(K.cpool.ap().tensor, 0, [[16, 128], [0, 512]])
    for k in range(KC):
        dst = cbf[0][:, k * 512:(k + 1) * 512]
        p.run("vector", tsfn(dst, zsrc, 0.0, ALU.mult, inc_col[:, k:k + 1], ALU.add))
        p.run("vector", tsfn(cmv[:, k * 512:(k + 1) * 512], zsrc, 0.0, ALU.mult,
                             inc_col[:, k:k + 1], ALU.add), inc=sve)
    c0_done = sve.n

    xln_ready = {}
    xln_spilled = {}

    def produce(bt, first=False):
        s = bt % 3
        def dst(k, s=s):
            return xln[:, (s * KC + k) * 512:(s * KC + k + 1) * 512]
        tw = (stats_sem, stats_n) if first else None
        if bt >= 3:
            p.wait("vector", spe, spe_base + 8 * (bt - 2))
            p.wait("vector", ssp[bt % 3], ssp[bt % 3].n)
        sem, cnt = emit_xln_batch(K, x_in, bt, dst, tag, tensor_wait=tw)
        xln_ready[bt] = (sem, cnt)
        p.wait("sync", sem, cnt)
        for k in range(KC):
            p.dma("sync", xln_hbm[k * 128:(k + 1) * 128, bt * 512:(bt + 1) * 512],
                  dst(k), ssp[bt % 3])

    produce(0, first=True)
    produce(1)

    gatesets = (("wfT", "ufT", AF.Sigmoid, 0), ("wiT", "uiT", AF.Sigmoid, 1),
                ("wcT", "ucT", AF.Tanh, 2))
    wave = 0
    for t in range(N_STEPS):
        ping = cbf[t % 2]
        pong = cbf[(t + 1) % 2]
        if t + 2 < B and not NO_KVPS_BANKS:
            produce(t + 2)
        for m in range(KC):
            if wave % 2 == 0:
                bankset = [K.pb[0][:, :], K.pb[1][:, :], K.pb[2][:, :]]
            elif NO_KVPS_BANKS:
                bankset = [K.pb[3][:, :], K.pb[4][:, :], K.pb[5][:, :]]
            else:
                bankset = [K.pb[3][:, :], K.kvps[:, 0:512], K.kvps[:, 512:1024]]
            sem, cnt = xln_ready[t]
            p.wait("tensor", sem, cnt)
            if SERIAL_WAVES == 1:
                p.wait("tensor", sse, sse.n)
                p.wait("tensor", sve, sve.n)
            elif SERIAL_WAVES == 2:
                p.wait("tensor", sse, sse.n)
            elif SERIAL_WAVES == 3:
                p.wait("tensor", sve, sve.n)
            if wave >= 2:
                p.wait("tensor", sse, sse_base + 3 * (wave - 1))
            if m == 0:
                p.wait("tensor", sve, c0_done + 25 * t)
                if t == 0:
                    p.wait("tensor", swl, wl_total)
            for (wn, un, _af, gi) in gatesets:
                bank = bankset[gi]
                for k in range(KC):
                    lhsT = gw[wn][:, k * 1024 + m * 128:k * 1024 + (m + 1) * 128]
                    rhs = xln[:, ((t % 3) * KC + k) * 512:((t % 3) * KC + k + 1) * 512]
                    p.run("tensor", mmfn(bank, lhsT, rhs, k == 0, False))
                for k in range(KC):
                    lhsT = gw[un][:, k * 1024 + m * 128:k * 1024 + (m + 1) * 128]
                    rhs = cmv[:, k * 512:(k + 1) * 512]
                    p.run("tensor", mmfn(bank, lhsT, rhs, False, k == KC - 1),
                          inc=(spe if (k == KC - 1 and gi == 2) else None))
            stg = fig[wave % 2]
            p.wait("scalar", spe, spe.n)
            if wave >= 2:
                p.wait("scalar", sve, c0_done + 3 * (wave - 1) + (wave // 8))
            for (wn, un, af, gi) in gatesets:
                bcol = bcols[:, gi * KC + m:gi * KC + m + 1]
                p.run("scalar", actfn(stg[:, gi * 512:(gi + 1) * 512],
                                      bankset[gi], af, bias=bcol), inc=sse)
            p.wait("vector", sse, sse.n)
            cold = ping[:, m * 512:(m + 1) * 512]
            cnew = pong[:, m * 512:(m + 1) * 512]
            p.run("vector", ttfn(cscr[:, 0:512], stg[:, 0:512], cold, ALU.mult))
            p.run("vector", ttfn(cscr[:, 512:1024], stg[:, 512:1024],
                                 stg[:, 1024:1536], ALU.mult))
            p.run("vector", ttfn(cnew, cscr[:, 0:512], cscr[:, 512:1024], ALU.add),
                  inc=sve, inc_k=3)
            wave += 1
        if t + 1 < B:
            for k in range(KC):
                p.run("vector", cpfn(cmv[:, k * 512:(k + 1) * 512],
                                     pong[:, k * 512:(k + 1) * 512]),
                      inc=(sve if k == KC - 1 else None))
            cmv_done = sve.n
    cfin = cbf[N_STEPS % 2]
    p.wait("vector", sve, sve.n)
    for k in range(KC):
        p.run("vector", cpfn(cmv[:, k * 512:(k + 1) * 512],
                             cfin[:, k * 512:(k + 1) * 512]),
              inc=(sve if k == KC - 1 else None))
    rec_pe, rec_se, rec_ve = spe.n, sse.n, sve.n
    if DEBUG_CDUMP is not None:
        sdbg = st("_dbg")
        p.wait("sync", sve, rec_ve)
        p.dma("sync", DEBUG_CDUMP, cfin[:, :], sdbg)
        p.wait("sync", sdbg, sdbg.n)
    if SKIP_H:
        if DEBUG_XDUMP is not None:
            p.wait("sync", sve, rec_ve)
            sdbg2 = st("_dbg2")
            p.dma("sync", DEBUG_XDUMP[0], cbf[0][:, :], sdbg2)
            p.dma("sync", DEBUG_XDUMP[1], fig[0][:, :], sdbg2)
            p.dma("sync", DEBUG_XDUMP[2], fig[1][:, :], sdbg2)
            p.dma("sync", DEBUG_XDUMP[3], inc_col[:, :], sdbg2)
            p.dma("sync", DEBUG_XDUMP[4],
                  xln[:, 0:KC * 512], sdbg2)
            p.wait("sync", sdbg2, sdbg2.n)
        return []

    p.wait("gpsimd", spe, rec_pe)
    for nm, sb in (("woT", woT), ("uoT", uoT)):
        for k in range(KC):
            p.dma("gpsimd", sb[:, k * 1024:(k + 1) * 1024],
                  wd[nm][k * 128:(k + 1) * 128, :], shl)
    hl_total = shl.n

    xlnring = Ring(K, "frld", 2)
    xrring = Ring(K, "fres", 4)
    xoring = Ring(K, "fout", 4)
    hb = 0
    for rt in range(B):
        s = rt % 2
        slot, freed, sxl2 = xlnring.acquire()
        if freed is not None:
            p.wait("sync", *freed)
        elif rt < 2:
            p.wait("sync", spe, rec_pe)
            for sp_ in ssp:
                p.wait("sync", sp_, sp_.n)
        for k in range(KC):
            p.dma("sync", xln[:, (s * KC + k) * 512:(s * KC + k + 1) * 512],
                  xln_hbm[k * 128:(k + 1) * 128, rt * 512:(rt + 1) * 512], sxl2)
        xln_loaded = sxl2.n
        for m in range(KC):
            bank = K.pb[hb % 2]
            if hb >= 2:
                p.wait("tensor", sse, rec_se + (hb - 1))
            if hb == 0:
                p.wait("tensor", shl, hl_total)
                p.wait("tensor", sve, rec_ve)
            p.wait("tensor", sxl2, xln_loaded)
            for k in range(KC):
                lhsT = woT[:, k * 1024 + m * 128:k * 1024 + (m + 1) * 128]
                rhs = xln[:, (s * KC + k) * 512:(s * KC + k + 1) * 512]
                p.run("tensor", mmfn(bank[:, :], lhsT, rhs, k == 0, False))
            for k in range(KC):
                lhsT = uoT[:, k * 1024 + m * 128:k * 1024 + (m + 1) * 128]
                rhs = cmv[:, k * 512:(k + 1) * 512]
                p.run("tensor", mmfn(bank[:, :], lhsT, rhs, False, k == KC - 1),
                      inc=(spe if k == KC - 1 else None))
            rslot, rfreed, srl = xrring.acquire()
            xr = xres[:, rslot * 512:(rslot + 1) * 512]
            if rfreed is not None:
                p.wait("sync", *rfreed)
            p.dma("sync", xr, x_in[m * 128:(m + 1) * 128, rt * 512:(rt + 1) * 512],
                  srl)
            p.wait("scalar", spe, spe.n)
            if hb >= 2:
                p.wait("scalar", shv, shv_base + hb - 1)
            bcol = bcols[:, 3 * KC + m:3 * KC + m + 1]
            p.run("scalar", actfn(hsb[hb % 2][:, :], bank[:, :], AF.Tanh, bias=bcol),
                  inc=sse)
            oslot, ofreed, sso = xoring.acquire()
            xo = xost[:, oslot * 512:(oslot + 1) * 512]
            if ofreed is not None:
                p.wait("vector", *ofreed)
            p.wait("vector", sse, sse.n)
            p.wait("vector", srl, srl.n)
            p.run("vector", ttfn(xo, hsb[hb % 2][:, :], xr, ALU.add), inc=shv)
            xrring.release(rslot, shv, shv.n)
            p.wait("sync", shv, shv.n)
            p.dma("sync", x_out[m * 128:(m + 1) * 128, rt * 512:(rt + 1) * 512],
                  xo, sso)
            xoring.release(oslot, sso, sso.n)
            hb += 1
        xlnring.release(slot, spe, spe.n)
    return [(s_, s_.n) for s_ in xoring.sems]



def host_const_arrays():
    return {
        "c_ones_col": np.ones((128, 1), np.float32),
        "c_ones_row": np.ones((1, 128), np.float32),
        "c_ones_row_bf": np.ones((1, 128), np.float32),
        "c_ident": np.eye(128, dtype=np.float32),
    }


def emit_const_loads(K, dr):
    p = K.p
    sc = K.sem("consts")
    p.dma("gpsimd", K.ones_col_f32[:, :], dr["c_ones_col"].ap(), sc)
    p.dma("gpsimd", K.ones_row_f32[:, :], dr["c_ones_row"].ap(), sc)
    p.dma("gpsimd", K.ones_row_bf[:, :], dr["c_ones_row_bf"].ap(), sc)
    p.dma("gpsimd", K.ones_col_bf[:, :], dr["c_ones_col"].ap(), sc)
    p.dma("gpsimd", K.ident_bf[:, :], dr["c_ident"].ap(), sc)
    p.dma("gpsimd", K.cpool[:, :], dr["c_cpool"].ap(), sc)
    tot = sc.n
    for eng in ENGINES:
        p.wait(eng, sc, tot)



def emit_attn_a(K, x_in, x_out, wdr, cc_in, cc_out, replica_groups, tag):
    p, nc = K.p, K.nc
    at = nc.alloc_sbuf_tensor_at
    st = lambda s: K.sem(tag + s)
    swl = st("_wl")
    aq = st("_pe")
    ase = st("_se")
    ave = st("_ve")
    scc = st("_cc")
    sccd = st("_ccd")
    sksl = st("_ks")
    srl = st("_rl")
    off = ARENA

    def A(name, shape, dt):
        nonlocal off
        t = at(tag + name, shape, dt, offset=off)
        nbytes = shape[1] * (2 if dt == BF16 else 4)
        off += (nbytes + 31) & ~31
        return t

    WqPMT = A("wqp", [128, KC * 1024], BF16)
    WkPMT = A("wkp", [128, KC * 1024], BF16)
    WvT = A("wvt", [128, KC * 1024], BF16)
    qp = A("qp", [128, 32 * 1024], BF16)
    kvd = A("kvd", [128, 64 * 64], F32)
    ksum_sb = A("ksm", [128, 64], F32)
    bproj_col = A("bpj", [128, KC], F32)
    regx = off
    xa = A("xa", [128, 2 * KC * 512], BF16)
    kb = A("kb", [128, 4 * 1024], BF16)
    vb = A("vb", [128, 4 * 1024], BF16)
    negmax = A("nmx", [128, 32], F32)
    bcol = A("bcl", [128, 32], F32)
    bqpm = A("bqp", [1, 2048], BF16)
    bv_bf = A("bvb", [1, 1024], BF16)
    pass1_end = off
    off = regx
    WprojT = A("wpj", [128, KC * 1024], BF16)
    ksrow = A("ksr", [1, 1024], F32)
    kvbd = A("kbd", [128, 2 * 128], BF16)
    ntm = A("ntm", [128, 16], F32)
    nrm2 = A("nr2", [128, 16], F32)
    rnorm = A("rnm", [128, 16], F32)
    nscr = A("nsc", [128, 64], F32)
    attn = A("att", [128, KC * 512], BF16)
    xres = A("xrs", [128, 3 * 512], F32)
    xost = A("xos", [128, 3 * 512], F32)
    off = max(off, pass1_end)
    assert off <= 229300, off
    qpoff = nc.lookup_mloc(qp).addr
    Wq_nat = at(tag + "wqn", [128, KC * 1024], BF16, offset=qpoff)
    Wk_nat = at(tag + "wkn", [128, KC * 1024], BF16, offset=qpoff + KC * 2048)
    pmT2 = at(tag + "pmt", [128, 64], BF16, offset=qpoff + 2 * KC * 2048)
    bq_st = at(tag + "bqs", [128, 32], BF16, offset=qpoff + 2 * KC * 2048 + 128)

    for k in range(KC):
        p.dma("gpsimd", Wq_nat[:, k * 1024:(k + 1) * 1024],
              wdr["Wq_nat"][k * 128:(k + 1) * 128, :], swl)
        p.dma("gpsimd", Wk_nat[:, k * 1024:(k + 1) * 1024],
              wdr["Wk_nat"][k * 128:(k + 1) * 128, :], swl)
        p.dma("gpsimd", WvT[:, k * 1024:(k + 1) * 1024],
              wdr["WvT"][k * 128:(k + 1) * 128, :], swl)
    p.dma("gpsimd", pmT2[:, :], wdr["pmT2"], swl)
    p.dma("gpsimd", bq_st[0:64, 0:16],
          bass.AP(wdr["b_qkv"].tensor, 0, [[1, 64], [64, 16]]), swl, slow=True)
    p.dma("gpsimd", bq_st[0:64, 16:32],
          bass.AP(wdr["b_qkv"].tensor, 1024, [[1, 64], [64, 16]]), swl, slow=True)
    p.dma("gpsimd", bv_bf[0:1, :],
          bass.AP(wdr["b_qkv"].tensor, 2048, [[1, 1], [1, 1024]]), swl)
    p.dma("gpsimd", bproj_col[:, :],
          bass.AP(wdr["b_proj"].tensor, 0, [[1, 128], [128, KC]]), swl, slow=True)
    wl_total = swl.n

    stats_sem, stats_n = emit_ln_stats(K, x_in, tag + "s")

    if ATTN_STAGE == 1:
        return []
    p.wait("tensor", swl, wl_total)
    prep_pe0 = aq.n
    for src_nat, dst, bofs in ((Wq_nat, WqPMT, 0), (Wk_nat, WkPMT, 1024)):
        for h in range(HEADS):
            tile = h // 2
            hh = h % 2
            for dc in range(KC):
                lhsT = src_nat[hh * 64:(hh + 1) * 64,
                               tile * 1024 + dc * 128: tile * 1024 + (dc + 1) * 128]
                rhs = pmT2[hh * 64:(hh + 1) * 64, :]
                bank = K.pb[(h * KC + dc) % 2]
                if aq.n >= prep_pe0 + 2:
                    p.wait("tensor", ave, ave.n - 1)
                p.run("tensor", mmfn(bank[:, 0:64], lhsT, rhs, True, True), inc=aq)
                p.wait("vector", aq, aq.n)
                p.run("vector", cpfn(dst[:, dc * 1024 + h * 64: dc * 1024 + h * 64 + 64],
                                     bank[:, 0:64]), inc=ave)
        boff16 = 0 if bofs == 0 else 16
        for h in range(HEADS):
            lhsT = bq_st[0:64, boff16 + h: boff16 + h + 1]
            rhs = pmT2[0:64, :]
            bank = K.pb[h % 2]
            p.wait("tensor", ave, ave.n)
            p.run("tensor", mmfn(bank[0:1, 0:64], lhsT, rhs, True, True), inc=aq)
            p.wait("vector", aq, aq.n)
            p.run("vector", cpfn(bqpm[0:1, bofs + h * 64: bofs + h * 64 + 64],
                                 bank[0:1, 0:64]), inc=ave)
    prep_ve = ave.n
    if ATTN_STAGE == 2:
        return []

    xln_ready = {}

    def produce(bt, first=False):
        s = bt % 2
        def dst(k, s=s):
            return xa[:, (s * KC + k) * 512:(s * KC + k + 1) * 512]
        if first:
            tw = (stats_sem, stats_n)
        else:
            tw = (ave, vb_evacs.get(4 * (bt - 1) - 1, 0))
        if bt >= 2:
            p.wait("vector", aq, batch_done[bt - 2])
        sem, cnt = emit_xln_batch(K, x_in, bt, dst, tag, tensor_wait=tw,
                                  banks=(2, 3))
        xln_ready[bt] = (sem, cnt)

    batch_done = {}
    vb_evacs = {}
    ksum_evac = {}
    sxv = K.sem(tag + "_xve")
    produce(0, first=True)
    for b in range(B):
        s = b % 2
        if b + 1 < B:
            produce(b + 1)
        sem, cnt = xln_ready[b]
        for nb in range(4):
            blk = b * 4 + nb
            p.wait("tensor", sem, cnt)
            if blk > 0:
                p.wait("tensor", ase, ase.n)
            else:
                p.wait("tensor", ave, prep_ve)
            p.run("tensor", mmfn(K.pb[0][:, :], K.ones_row_bf[:, :],
                                 bqpm[0:1, 0:512], True, False))
            p.run("tensor", mmfn(K.pb[1][:, :], K.ones_row_bf[:, :],
                                 bqpm[0:1, 512:1024], True, False))
            for k in range(KC):
                lhsT = xa[:, (s * KC + k) * 512 + nb * 128:
                          (s * KC + k) * 512 + (nb + 1) * 128]
                p.run("tensor", mmfn(K.pb[0][:, :], lhsT, WqPMT[:, k * 1024:k * 1024 + 512],
                                     False, k == KC - 1))
                p.run("tensor", mmfn(K.pb[1][:, :], lhsT,
                                     WqPMT[:, k * 1024 + 512:(k + 1) * 1024],
                                     False, k == KC - 1), inc=(aq if k == KC - 1 else None))
            p.wait("vector", aq, aq.n)
            a0 = bass.AP(K.pb[0].ap().tensor, 0, [[512, 128], [64, 8], [1, 64]])
            a1 = bass.AP(K.pb[1].ap().tensor, 0, [[512, 128], [64, 8], [1, 64]])
            p.run("vector", redfn(negmax[:, 0:8], a0, ALU.max, negate=True))
            p.run("vector", redfn(negmax[:, 8:16], a1, ALU.max, negate=True), inc=ave)
            p.wait("vector", ave, ave.n)
            eps_b = bass.AP(K.cpool.ap().tensor, K.cidx(EPS_PERF), [[16, 128], [0, 16]])
            p.run("vector", sttfn(bcol[:, 0:16], negmax[:, 0:16], S_PERF,
                                  eps_b, ALU.mult, ALU.add), inc=ave)
            p.wait("scalar", ave, ave.n)
            for h in range(HEADS):
                bank = K.pb[0] if h < 8 else K.pb[1]
                sl = bank[:, (h % 8) * 64:(h % 8 + 1) * 64]
                p.run("scalar", actfn(qp[:, blk * 1024 + h * 64: blk * 1024 + (h + 1) * 64],
                                      sl, AF.Exp, bias=bcol[:, h:h + 1],
                                      scale=K.cv(S_PERF)), inc=ase)
            p.wait("tensor", ase, ase.n)
            p.run("tensor", mmfn(K.pb[0][:, :], K.ones_row_bf[:, :],
                                 bqpm[0:1, 1024:1536], True, False))
            p.run("tensor", mmfn(K.pb[1][:, :], K.ones_row_bf[:, :],
                                 bqpm[0:1, 1536:2048], True, False))
            for k in range(KC):
                lhsT = xa[:, (s * KC + k) * 512 + nb * 128:
                          (s * KC + k) * 512 + (nb + 1) * 128]
                p.run("tensor", mmfn(K.pb[0][:, :], lhsT, WkPMT[:, k * 1024:k * 1024 + 512],
                                     False, k == KC - 1))
                p.run("tensor", mmfn(K.pb[1][:, :], lhsT,
                                     WkPMT[:, k * 1024 + 512:(k + 1) * 1024],
                                     False, k == KC - 1), inc=(aq if k == KC - 1 else None))
            p.wait("vector", aq, aq.n)
            p.run("vector", redfn(negmax[:, 16:24], a0, ALU.max, negate=True))
            p.run("vector", redfn(negmax[:, 24:32], a1, ALU.max, negate=True), inc=ave)
            p.wait("vector", ave, ave.n)
            p.run("vector", sttfn(bcol[:, 16:32], negmax[:, 16:32], S_PERF,
                                  eps_b, ALU.mult, ALU.add), inc=ave)
            p.wait("scalar", ave, ave.n)
            for h in range(HEADS):
                bank = K.pb[0] if h < 8 else K.pb[1]
                sl = bank[:, (h % 8) * 64:(h % 8 + 1) * 64]
                p.run("scalar", actfn(kb[:, nb * 1024 + h * 64: nb * 1024 + (h + 1) * 64],
                                      sl, AF.Exp, bias=bcol[:, 16 + h:17 + h],
                                      scale=K.cv(S_PERF)), inc=ase)
            p.wait("tensor", sxv, sxv.n)
            if blk > 0:
                p.wait("tensor", ave, vb_evacs.get(blk - 1, 0))
            p.run("tensor", mmfn(K.pb[2][:, :], K.ones_row_bf[:, :],
                                 bv_bf[0:1, 0:512], True, False))
            p.run("tensor", mmfn(K.pb[3][:, :], K.ones_row_bf[:, :],
                                 bv_bf[0:1, 512:1024], True, False))
            for k in range(KC):
                lhsT = xa[:, (s * KC + k) * 512 + nb * 128:
                          (s * KC + k) * 512 + (nb + 1) * 128]
                p.run("tensor", mmfn(K.pb[2][:, :], lhsT, WvT[:, k * 1024:k * 1024 + 512],
                                     False, k == KC - 1))
                p.run("tensor", mmfn(K.pb[3][:, :], lhsT,
                                     WvT[:, k * 1024 + 512:(k + 1) * 1024],
                                     False, k == KC - 1), inc=(aq if k == KC - 1 else None))
            p.wait("vector", aq, aq.n)
            p.run("vector", cpfn(vb[:, nb * 1024:nb * 1024 + 512], K.pb[2][:, :]))
            p.run("vector", cpfn(vb[:, nb * 1024 + 512:(nb + 1) * 1024], K.pb[3][:, :]),
                  inc=ave)
            vb_evacs[blk] = ave.n
        p.wait("tensor", ase, ase.n)
        p.wait("tensor", ave, ave.n)
        if b > 0:
            p.wait("tensor", ave, ksum_evac[b - 1])
        for hp in range(KC):
            for nb in range(4):
                lhsT = kb[:, nb * 1024 + hp * 128: nb * 1024 + (hp + 1) * 128]
                p.run("tensor", mmfn(K.kvps[:, hp * 128:(hp + 1) * 128], lhsT,
                                     vb[:, nb * 1024 + hp * 128: nb * 1024 + (hp + 1) * 128],
                                     nb == 0, nb == 3))
            for nb in range(4):
                lhsT = kb[:, nb * 1024 + hp * 128: nb * 1024 + (hp + 1) * 128]
                p.run("tensor", mmfn(K.pb[4][:, hp:hp + 1], lhsT,
                                     K.ones_col_bf[:, :], nb == 0, nb == 3),
                      inc=(aq if hp == KC - 1 and nb == 3 else None))
        batch_done[b] = aq.n
        p.wait("vector", aq, aq.n)
        z = b * 8
        for hp in range(KC):
            p.run("vector", cpfn(kvd[0:64, (z + hp) * 64:(z + hp + 1) * 64],
                                 K.kvps[0:64, hp * 128:hp * 128 + 64]))
            p.run("vector", cpfn(kvd[64:128, (z + hp) * 64:(z + hp + 1) * 64],
                                 K.kvps[64:128, hp * 128 + 64:(hp + 1) * 128]))
        p.run("vector", cpfn(ksum_sb[:, b * 8:(b + 1) * 8], K.pb[4][:, 0:8]),
              inc=ave)
        ksum_evac[b] = ave.n

    if ATTN_STAGE == 3:
        return []
    p.wait("gpsimd", ave, ksum_evac[B - 1])
    p.dma("gpsimd", cc_in[:, 0:4096], kvd[:, :], sccd)
    p.dma("gpsimd", cc_in[:, 4096:4160], ksum_sb[:, :], sccd)
    p.wait("gpsimd", sccd, sccd.n)
    if NO_COLLECTIVE:
        p.dma("gpsimd", cc_out, cc_in, sccd)
        p.wait("gpsimd", sccd, sccd.n)
        p.run("gpsimd", lambda e: e.sem_inc(scc.h, 1))
        scc.n += 1
    else:
        def ccfn(e, i=cc_in, o=cc_out, rg=replica_groups):
            return e.collective_compute("AllReduce", ALU.add, replica_groups=rg,
                                        ins=[i.opt()], outs=[o.opt()])
        p.run("gpsimd", ccfn, inc=scc)
    p.wait("gpsimd", scc, scc.n)
    p.dma("gpsimd", kvd[:, :], cc_out[:, 0:4096], sccd)
    for k in range(KC):
        p.dma("gpsimd", WprojT[:, k * 1024:(k + 1) * 1024],
              wdr["WprojT"][k * 128:(k + 1) * 128, :], sccd)
    kv_back = sccd.n

    if ATTN_STAGE == 4:
        return []
    proj_mm_ve_l = []
    xrring = Ring(K, tag + "xr", 3)
    xoring = Ring(K, tag + "xo", 3)
    for b in range(B):
        if b >= 1:
            p.wait("gpsimd", aq, ksrow_used[b - 1])
        p.wait("gpsimd", scc, scc.n)
        ap = bass.AP(cc_out.tensor, 4096 + b * 8,
                     [[1, 1], [1, 8], [64 * 4160, 2], [4160, 64]])
        p.dma("gpsimd", ksrow[0:1, :], ap, sksl, slow=True)
        if b > 0:
            if ATTN_POST == 5:
                p.wait("tensor", ave, proj_ve[b - 1])
            else:
                p.wait("tensor", ave, ave.n)
        p.wait("tensor", sksl, sksl.n)
        p.run("tensor", mmfn(K.pb[2][:, :], K.ones_row_f32[:, :],
                             ksrow[0:1, 0:512], True, True))
        p.run("tensor", mmfn(K.pb[3][:, :], K.ones_row_f32[:, :],
                             ksrow[0:1, 512:1024], True, True), inc=aq)
        ksrow_used = getattr(K, tag + "_ksu", {})
        ksrow_used[b] = aq.n
        setattr(K, tag + "_ksu", ksrow_used)
        for nb in range(4):
            if ATTN_POST < 2:
                continue
            blk = b * 4 + nb
            if ATTN_POST >= 12 or ATTN_POST < 10:
                p.wait("vector", aq, aq.n)
                for h in range(HEADS):
                    bank = K.pb[2] if h < 8 else K.pb[3]
                    p.run("vector", sttfn(nscr[:, 0:64],
                                          qp[:, blk * 1024 + h * 64: blk * 1024 + (h + 1) * 64],
                                          1.0, bank[:, (h % 8) * 64:(h % 8 + 1) * 64],
                                          ALU.mult, ALU.mult, accum=ntm[:, h:h + 1]),
                          inc=(ave if h == HEADS - 1 else None))
            if ATTN_POST >= 13 or ATTN_POST < 10:
                p.wait("vector", ave, ave.n)
                p.run("vector", tsfn(nrm2[:, :], ntm[:, :], 1e-6, ALU.add), inc=ave)
                p.wait("vector", ave, ave.n)
                p.run("vector", lambda e, o=rnorm[:, :], i=nrm2[:, :]: e.reciprocal(o, i),
                      inc=ave)
            if ATTN_POST >= 14 or ATTN_POST < 10:
                p.wait("vector", ave, ave.n)
                rb = bass.AP(rnorm.ap().tensor, 0, [[16, 128], [1, 16], [0, 64]])
                p.run("vector", ttfn(qp[:, blk * 1024:(blk + 1) * 1024],
                                     qp[:, blk * 1024:(blk + 1) * 1024], rb, ALU.mult),
                      inc=ave)
            if ATTN_POST < 3 or ATTN_POST >= 10:
                continue
            p.wait("tensor", ave, ave.n)
            tr_evac = {}
            for hp in range(KC):
                sl = qp[:, blk * 1024 + hp * 128: blk * 1024 + (hp + 1) * 128]
                bankt = K.pb[4] if hp % 2 == 0 else K.pb[5]
                if hp >= 2:
                    p.wait("tensor", ave, tr_evac[hp - 2])
                p.run("tensor", mmfn(bankt[:, 0:128], sl, K.ident_bf[:, :],
                                     True, True), inc=aq)
                p.wait("vector", aq, aq.n)
                p.run("vector", cpfn(sl, bankt[:, 0:128]), inc=ave)
                tr_evac[hp] = ave.n
        if ATTN_POST < 4 or ATTN_POST >= 10:
            continue
        p.wait("gpsimd", sccd, kv_back)
        for hp in range(KC):
            sbd = kvbd[:, (hp % 2) * 128:(hp % 2 + 1) * 128]
            zb = bass.AP(K.cpool.ap().tensor, 0, [[16, 128], [0, 128]])
            if hp >= 2:
                p.wait("vector", aq, out_mm[hp - 2])
            p.wait("vector", sccd, kv_back)
            p.run("vector", tsfn(sbd, zb, 0.0, ALU.mult))
            z = (b * 8 + hp) * 64
            p.run("vector", cpfn(sbd[0:64, 0:64], kvd[0:64, z:z + 64]))
            p.run("vector", cpfn(sbd[64:128, 64:128], kvd[64:128, z:z + 64]), inc=ave)
            p.wait("tensor", ave, ave.n)
            if hp >= 2:
                p.wait("tensor", ase, out_evac.get(hp - 2, 0))
            mv = bass.AP(qp.ap().tensor, (b * 4) * 1024 + hp * 128,
                         [[32768, 128], [1024, 4], [1, 128]])
            bank = K.pb[0] if hp % 2 == 0 else K.pb[1]
            p.run("tensor", mmfn(bank[:, :], sbd, mv, True, True), inc=aq)
            out_mm = getattr(K, tag + "_omm", {})
            out_mm[hp] = aq.n
            setattr(K, tag + "_omm", out_mm)
            p.wait("scalar", aq, aq.n)
            p.run("scalar", actfn(attn[:, hp * 512:(hp + 1) * 512], bank[:, :],
                                  AF.Identity), inc=ase)
            out_evac = getattr(K, tag + "_oev", {})
            out_evac[hp] = ase.n
            setattr(K, tag + "_oev", out_evac)
        if ATTN_POST < 5 or ATTN_POST >= 10:
            continue
        for m in list(range(KC))[::-1] if PROJ_REV else range(KC):
            bank = K.pb[2] if m % 2 == 0 else K.pb[3]
            p.wait("tensor", ase, ase.n)
            if len(proj_mm_ve_l) >= 2:
                p.wait("tensor", ave, proj_mm_ve_l[-2])
            for k in range(KC):
                lhsT = WprojT[:, k * 1024 + m * 128: k * 1024 + (m + 1) * 128]
                p.run("tensor", mmfn(bank[:, :], lhsT, attn[:, k * 512:(k + 1) * 512],
                                     k == 0, k == KC - 1),
                      inc=(aq if k == KC - 1 else None))
            rslot, rfreed, srl2 = xrring.acquire()
            xr = xres[:, rslot * 512:(rslot + 1) * 512]
            if rfreed is not None:
                p.wait("sync", *rfreed)
            else:
                p.wait("sync", scc, scc.n)
            p.dma("sync", xr, x_in[m * 128:(m + 1) * 128, b * 512:(b + 1) * 512], srl2)
            oslot, ofreed, sso = xoring.acquire()
            xo = xost[:, oslot * 512:(oslot + 1) * 512]
            if ofreed is not None:
                p.wait("vector", *ofreed)
            p.wait("vector", aq, aq.n)
            p.wait("vector", srl2, srl2.n)
            p.run("vector", sttfn(xo, bank[:, :], bproj_col[:, m:m + 1], xr,
                                  ALU.add, ALU.add), inc=ave)
            proj_mm_ve_l.append(ave.n)
            if DEBUG_PROJ is not None and b == 0 and len(proj_mm_ve_l) == 1:
                sdbg = st("_pdbg")
                p.wait("vector", ave, ave.n)
                p.run("vector", cpfn(xost[:, 512:1024], bank[:, :]), inc=ave)
                p.wait("sync", ave, ave.n)
                p.dma("sync", DEBUG_PROJ[0], xost[:, 512:1024], sdbg)
                p.dma("sync", DEBUG_PROJ[1], xr, sdbg)
                p.dma("sync", DEBUG_PROJ[2], attn[:, :], sdbg)
                p.wait("sync", sdbg, sdbg.n)
            xrring.release(rslot, ave, ave.n)
            p.wait("sync", ave, ave.n)
            p.dma("sync", x_out[m * 128:(m + 1) * 128, b * 512:(b + 1) * 512], xo, sso)
            xoring.release(oslot, sso, sso.n)
        proj_ve = getattr(K, tag + "_pve", {})
        proj_ve[b] = ave.n
        setattr(K, tag + "_pve", proj_ve)
    return [(s_, s_.n) for s_ in xoring.sems]



def emit_attn_b(K, x_in, x_out, wdr, ag_in, ag_out, replica_groups, tag):
    p, nc = K.p, K.nc
    at = nc.alloc_sbuf_tensor_at
    st = lambda s: K.sem(tag + s)
    swl = st("_wl")
    bq = st("_pe")
    bse = st("_se")
    bve = st("_ve")
    scc = st("_cc")
    sccd = st("_ccd")
    sxl = st("_xl")
    ncores = len(replica_groups[0])
    off = ARENA

    def A(name, shape, dt):
        nonlocal off
        t = at(tag + name, shape, dt, offset=off)
        nbytes = shape[1] * (2 if dt == BF16 else 4)
        off += (nbytes + 31) & ~31
        return t

    BwT = A("bwt", [128, KC * 1024], BF16)
    Bx = A("bx", [128, 4 * 4096], F32)
    xlr = A("xlr", [128, 2 * KC * 512], BF16)
    Tt = A("T", [128, 516], F32)
    St = A("S", [128, 516], F32)
    tsc1 = A("t1", [128, 256], F32)
    tsc2 = A("t2", [128, 256], F32)
    gcs = A("gcs", [128, 4096], F32)
    gss = A("gss", [128, 4096], F32)
    Pb = A("P", [128, 512], F32)
    Qb = A("Q", [128, 512], F32)
    vv = A("vv", [128, 4096], F32)
    om = A("om", [128, 8], F32)
    w_ = A("w", [128, 8], F32)
    c1 = A("c1", [128, 8], F32)
    s1 = A("s1", [128, 8], F32)
    rs1 = A("rs1", [128, 8], F32)
    scl = A("scl", [128, 8], F32)
    sA = A("sA", [128, 8], F32)
    sB = A("sB", [128, 8], F32)
    locf = A("locf", [128, 128], F32)
    stj = A("stj", [128, 8 * 128], F32)
    agb = A("agb", [128, 8 * 128], F32)
    bg = A("bg", [128, 128], F32)
    bgi = A("bgi", [128, 128], F32)
    coresel = A("csl", [128, 8], F32)
    usb = A("usb", [128, 24], F32)
    xres = A("xrs", [128, 3 * 512], F32)
    xost = A("xos", [128, 3 * 512], F32)
    assert off <= 229300, off

    DT2 = DT_B * DT_B

    for k in range(KC):
        p.dma("gpsimd", BwT[:, k * 1024:(k + 1) * 1024],
              wdr["BwT"][k * 128:(k + 1) * 128, :], swl)
    p.dma("gpsimd", om[:, :], bass.AP(wdr["omega"].tensor, 0, [[1, 128], [128, KC]]),
          swl, slow=True)
    p.dma("gpsimd", coresel[:, :], wdr["coresel"], swl)
    wl_total = swl.n

    def V(fn):
        p.run("vector", fn, inc=bve)
        p.wait("vector", bve, bve.n)

    def SCA(fn):
        p.wait("scalar", bve, bve.n)
        p.run("scalar", fn, inc=bse)
        p.wait("vector", bse, bse.n)

    p.wait("vector", swl, wl_total)
    p.wait("scalar", swl, wl_total)
    SCA(actfn(sA[:, :], om[:, :], AF.Abs))
    SCA(actfn(sA[:, :], sA[:, :], AF.Exp, scale=K.cv(-1.0)))
    V(tsfn(sA[:, :], sA[:, :], 1.0, ALU.add))
    SCA(actfn(sA[:, :], sA[:, :], AF.Ln))
    V(tsfn(sB[:, :], om[:, :], 0.0, ALU.max))
    V(ttfn(w_[:, :], sA[:, :], sB[:, :], ALU.add))
    V(tsfn(sA[:, :], w_[:, :], DT2 / 2, ALU.mult))
    V(tsfn(c1[:, :], sA[:, :], -1.0, ALU.mult))
    V(tsfn(c1[:, :], c1[:, :], 1.0, ALU.add))
    V(tsfn(sB[:, :], sA[:, :], -1.0, ALU.mult))
    V(tsfn(sB[:, :], sB[:, :], 2.0, ALU.add))
    V(ttfn(sB[:, :], sA[:, :], sB[:, :], ALU.mult))
    SCA(actfn(s1[:, :], sB[:, :], AF.Sqrt))
    V(lambda e, o=rs1[:, :], i=s1[:, :]: e.reciprocal(o, i))
    V(ttfn(sA[:, :], sB[:, :], rs1[:, :], ALU.mult))
    V(ttfn(s1[:, :], s1[:, :], sA[:, :], ALU.add))
    V(tsfn(s1[:, :], s1[:, :], 0.5, ALU.mult))
    V(lambda e, o=rs1[:, :], i=s1[:, :]: e.reciprocal(o, i))
    V(tsfn(scl[:, :], rs1[:, :], DT2, ALU.mult))
    prep_ve = bve.n

    stats_sem, stats_n = emit_ln_stats(K, x_in, tag + "s")
    xln_hbm = K.xln_hbm
    sxs = st("_xs")
    for bt in range(B):
        s_ = bt % 2
        def dst(kk, s_=s_):
            return xlr[:, (s_ * KC + kk) * 512:(s_ * KC + kk + 1) * 512]
        tw = (stats_sem, stats_n) if bt == 0 else None
        if bt >= 2:
            p.wait("vector", sxs, spill_mark[bt - 2])
        sem, cnt = emit_xln_batch(K, x_in, bt, dst, tag, tensor_wait=tw)
        p.wait("sync", sem, cnt)
        for kk in range(KC):
            p.dma("sync", xln_hbm[kk * 128:(kk + 1) * 128, bt * 512:(bt + 1) * 512],
                  dst(kk), sxs)
        spill_mark = getattr(K, tag + "_spm", {})
        spill_mark[bt] = sxs.n
        setattr(K, tag + "_spm", spill_mark)
    spills_done = sxs.n

    def build_tables(c):
        zsrc1 = bass.AP(K.cpool.ap().tensor, 0, [[16, 128], [0, 1]])
        V(tsfn(Tt[:, 0:1], zsrc1, 0.0, ALU.mult, K.cv(1.0), ALU.add))
        V(tsfn(St[:, 0:1], zsrc1, 0.0, ALU.mult))
        V(cpfn(Tt[:, 1:2], c1[:, c:c + 1]))
        V(cpfn(St[:, 1:2], s1[:, c:c + 1]))
        L = 1
        while L < 512:
            n = min(L, 512 - L)
            cL = Tt[:, L:L + 1]
            sL = St[:, L:L + 1]
            V(tsfn(tsc1[:, 0:n], St[:, 1:1 + n], sL, ALU.mult))
            V(tsfn(tsc2[:, 0:n], Tt[:, 1:1 + n], sL, ALU.mult))
            V(sttfn(Tt[:, L + 1:L + 1 + n], Tt[:, 1:1 + n], cL, tsc1[:, 0:n],
                    ALU.mult, ALU.subtract))
            V(sttfn(St[:, L + 1:L + 1 + n], St[:, 1:1 + n], cL, tsc2[:, 0:n],
                    ALU.mult, ALU.add))
            L += n
        cL = Tt[:, 1:2]
        sL = St[:, 1:2]
        V(tsfn(tsc1[:, 0:1], St[:, 512:513], sL, ALU.mult))
        V(tsfn(tsc2[:, 0:1], Tt[:, 512:513], sL, ALU.mult))
        V(sttfn(Tt[:, 513:514], Tt[:, 512:513], cL, tsc1[:, 0:1],
                ALU.mult, ALU.subtract))
        V(sttfn(St[:, 513:514], St[:, 512:513], cL, tsc2[:, 0:1],
                ALU.mult, ALU.add))

    def produce_bx(hf, mark):
        xring = bxring
        for rt in range(B):
            slot, freed, sl = xring.acquire()
            if freed is not None:
                p.wait("sync", *freed)
            else:
                p.wait("sync", bq, bq.n)
                if mark == 0:
                    p.wait("sync", sxs, spills_done)
            for kk in range(KC):
                p.dma("sync", xlr[:, (slot * KC + kk) * 512:(slot * KC + kk + 1) * 512],
                      xln_hbm[kk * 128:(kk + 1) * 128, rt * 512:(rt + 1) * 512], sl)
            for ci in range(4):
                c = 4 * hf + ci
                bank = K.pb[ci]
                if True:
                    p.wait("tensor", bve, bx_evac.get((mark, rt - 1, ci), 0))
                p.wait("tensor", sl, sl.n)
                p.wait("tensor", swl, wl_total)
                for kk in range(KC):
                    lhsT = BwT[:, kk * 1024 + c * 128: kk * 1024 + (c + 1) * 128]
                    mv = xlr[:, (slot * KC + kk) * 512:(slot * KC + kk + 1) * 512]
                    p.run("tensor", mmfn(bank[:, :], lhsT, mv, kk == 0, kk == KC - 1),
                          inc=(bq if kk == KC - 1 else None))
                p.wait("vector", bq, bq.n)
                p.run("vector", cpfn(Bx[:, ci * 4096 + rt * 512: ci * 4096 + (rt + 1) * 512],
                                     bank[:, :]), inc=bve)
                bx_evac[(mark, rt, ci)] = bve.n
    bx_evac = {}
    bxring = Ring(K, "bx", 2)

    for hf in (0, 1):
        produce_bx(hf, 0 if hf == 0 else 1)
        for ci in range(4):
            c = 4 * hf + ci
            p.wait("vector", bve, bx_evac[(0 if hf == 0 else 1, B - 1, ci)])
            build_tables(c)
            V(tsfn(usb[:, c * 3:c * 3 + 1], St[:, 513:514], rs1[:, c:c + 1], ALU.mult))
            V(tsfn(usb[:, c * 3 + 1:c * 3 + 2], St[:, 512:513], rs1[:, c:c + 1],
                   ALU.mult))
            V(tsfn(usb[:, c * 3 + 2:c * 3 + 3], St[:, 511:512], rs1[:, c:c + 1],
                   ALU.mult))
            cglob = c
            for b in range(B):
                bxs = Bx[:, ci * 4096 + b * 512: ci * 4096 + (b + 1) * 512]
                p.run("vector", sttfn(gcs[:, 0:512], bxs, 1.0, Tt[:, 0:512],
                                      ALU.mult, ALU.mult, accum=sA[:, 0:1]))
                p.run("vector", sttfn(gss[:, 0:512], bxs, 1.0, St[:, 0:512],
                                      ALU.mult, ALU.mult, accum=sA[:, 1:2]), inc=bve)
                p.wait("vector", bve, bve.n)
                V(tsfn(sA[:, 2:3], bxs[:, 511:512], Tt[:, 511:512], ALU.mult))
                V(tsfn(sA[:, 3:4], bxs[:, 511:512], St[:, 511:512], ALU.mult))
                V(ttfn(sA[:, 2:3], sA[:, 0:1], sA[:, 2:3], ALU.subtract))
                V(ttfn(sA[:, 3:4], sA[:, 1:2], sA[:, 3:4], ALU.subtract))
                V(tsfn(sB[:, 0:1], sA[:, 0:1], St[:, 512:513], ALU.mult))
                V(tsfn(sB[:, 1:2], sA[:, 1:2], Tt[:, 512:513], ALU.mult))
                V(ttfn(sB[:, 0:1], sB[:, 0:1], sB[:, 1:2], ALU.subtract))
                V(tsfn(locf[:, (cglob * 8 + b) * 2:(cglob * 8 + b) * 2 + 1],
                       sB[:, 0:1], scl[:, c:c + 1], ALU.mult))
                V(tsfn(sB[:, 2:3], sA[:, 2:3], St[:, 511:512], ALU.mult))
                V(tsfn(sB[:, 3:4], sA[:, 3:4], Tt[:, 511:512], ALU.mult))
                V(ttfn(sB[:, 2:3], sB[:, 2:3], sB[:, 3:4], ALU.subtract))
                V(tsfn(locf[:, (cglob * 8 + b) * 2 + 1:(cglob * 8 + b) * 2 + 2],
                       sB[:, 2:3], scl[:, c:c + 1], ALU.mult))

    p.wait("gpsimd", bve, bve.n)
    p.dma("gpsimd", ag_in, locf[:, :], sccd)
    p.wait("gpsimd", sccd, sccd.n)
    def ccfn(e, i=ag_in, o=ag_out, rg=replica_groups):
        return e.collective_compute("AllGather", ALU.bypass, replica_groups=rg,
                                    ins=[i.opt()], outs=[o.opt()])
    p.run("gpsimd", ccfn, inc=scc)
    p.wait("gpsimd", scc, scc.n)
    for j in range(ncores):
        p.dma("gpsimd", agb[:, j * 128:(j + 1) * 128],
              ag_out[j * 128:(j + 1) * 128, :], sccd)
    ag_back = sccd.n

    p.wait("vector", sccd, ag_back)
    zsrcN = bass.AP(K.cpool.ap().tensor, 0, [[16, 128], [0, 128]])
    V(tsfn(stj[:, 0:128], zsrcN, 0.0, ALU.mult))
    def ubc(i):
        return bass.AP(usb.ap().tensor, i, [[24, 128], [3, 8], [0, 8]])
    for j in range(ncores - 1):
        ste = bass.AP(stj.ap().tensor, j * 128, [[1024, 128], [2, 64]])
        sto = bass.AP(stj.ap().tensor, j * 128 + 1, [[1024, 128], [2, 64]])
        age = bass.AP(agb.ap().tensor, j * 128, [[1024, 128], [2, 64]])
        ago = bass.AP(agb.ap().tensor, j * 128 + 1, [[1024, 128], [2, 64]])
        nxe = bass.AP(stj.ap().tensor, (j + 1) * 128, [[1024, 128], [2, 64]])
        nxo = bass.AP(stj.ap().tensor, (j + 1) * 128 + 1, [[1024, 128], [2, 64]])
        tc1 = tsc1[:, 0:64]
        tc2 = tsc2[:, 0:64]
        V(ttfn(tc1, ste, ubc(0), ALU.mult))
        V(ttfn(tc2, sto, ubc(1), ALU.mult))
        V(ttfn(tc1, tc1, tc2, ALU.subtract))
        V(ttfn(nxe, tc1, age, ALU.add))
        V(ttfn(tc1, ste, ubc(1), ALU.mult))
        V(ttfn(tc2, sto, ubc(2), ALU.mult))
        V(ttfn(tc1, tc1, tc2, ALU.subtract))
        V(ttfn(nxo, tc1, ago, ALU.add))
    V(tsfn(bg[:, 0:128], stj[:, 0:128], coresel[:, 0:1], ALU.mult))
    for j in range(1, ncores):
        V(sttfn(bg[:, 0:128], stj[:, j * 128:(j + 1) * 128], coresel[:, j:j + 1],
                bg[:, 0:128], ALU.mult, ALU.add))
    vin = bass.AP(bg.ap().tensor, 0, [[128, 128], [2, 64]])
    vin2 = bass.AP(bg.ap().tensor, 1, [[128, 128], [2, 64]])
    c1b = bass.AP(c1.ap().tensor, 0, [[8, 128], [1, 8], [0, 8]])
    s1b = bass.AP(s1.ap().tensor, 0, [[8, 128], [1, 8], [0, 8]])
    V(ttfn(bgi[:, 0:64], vin, c1b, ALU.mult))
    V(ttfn(bgi[:, 0:64], bgi[:, 0:64], vin2, ALU.subtract))
    V(ttfn(bgi[:, 64:128], vin, s1b, ALU.mult))
    V(tsfn(bgi[:, 64:128], bgi[:, 64:128], -1.0, ALU.mult))
    V(tsfn(bgi[:, 0:64], bgi[:, 0:64], 1.0 / DT2, ALU.mult))
    V(tsfn(bgi[:, 64:128], bgi[:, 64:128], 1.0 / DT2, ALU.mult))
    chain_ve = bve.n

    xrring = Ring(K, tag + "xr", 3)
    xoring = Ring(K, tag + "xo", 3)
    Tb = bass.AP(Tt.ap().tensor, 0, [[516, 128], [0, 8], [1, 512]])
    Sb = bass.AP(St.ap().tensor, 0, [[516, 128], [0, 8], [1, 512]])
    for hf in (0, 1):
        produce_bx(hf, 2 + hf)
        for ci in range(4):
            c = 4 * hf + ci
            p.wait("vector", bve, bx_evac[(2 + hf, B - 1, ci)])
            build_tables(c)
            p.run("vector", ttfn(gcs[:, :], Bx[:, ci * 4096:(ci + 1) * 4096], Tb,
                                 ALU.mult))
            p.run("vector", ttfn(gss[:, :], Bx[:, ci * 4096:(ci + 1) * 4096], Sb,
                                 ALU.mult), inc=bve)
            p.wait("vector", bve, bve.n)
            for b in range(B):
                sl0 = slice(b * 512, (b + 1) * 512)
                onesb = bass.AP(K.cpool.ap().tensor, K.cidx(1.0), [[16, 128], [0, 512]])
                p.run("vector", scanfn(Pb[:, :], onesb, gcs[:, sl0],
                                       bgi[:, c * 8 + b:c * 8 + b + 1],
                                       ALU.mult, ALU.add))
                p.run("vector", scanfn(Qb[:, :], onesb, gss[:, sl0],
                                       bgi[:, 64 + c * 8 + b:64 + c * 8 + b + 1],
                                       ALU.mult, ALU.add))
                p.run("vector", ttfn(Pb[:, :], Pb[:, :], St[:, 1:513], ALU.mult))
                p.run("vector", ttfn(Qb[:, :], Qb[:, :], Tt[:, 1:513], ALU.mult))
                p.run("vector", ttfn(Pb[:, :], Pb[:, :], Qb[:, :], ALU.subtract))
                p.run("vector", tsfn(Pb[:, :], Pb[:, :], scl[:, c:c + 1], ALU.mult),
                      inc=bve)
                rslot, rfreed, srl2 = xrring.acquire()
                xr = xres[:, rslot * 512:(rslot + 1) * 512]
                if rfreed is not None:
                    p.wait("sync", *rfreed)
                p.dma("sync", xr, x_in[c * 128:(c + 1) * 128, sl0], srl2)
                oslot, ofreed, sso = xoring.acquire()
                xo = xost[:, oslot * 512:(oslot + 1) * 512]
                if ofreed is not None:
                    p.wait("vector", *ofreed)
                p.wait("vector", srl2, srl2.n)
                p.run("vector", ttfn(xo, Pb[:, :], xr, ALU.add), inc=bve)
                xrring.release(rslot, bve, bve.n)
                p.wait("sync", bve, bve.n)
                p.dma("sync", x_out[c * 128:(c + 1) * 128, sl0], xo, sso)
                xoring.release(oslot, sso, sso.n)
    return [(s_, s_.n) for s_ in xoring.sems]



_NC_CACHE = {}


def build_full_nc():
    if "nc" in _NC_CACHE:
        return _NC_CACHE["nc"], _NC_CACHE["cpool"]
    nc = bass.Bass("TRN2", target_bir_lowering=False, debug=False,
                   detect_race_conditions=False)
    K = Kern(nc)
    x_in = nc.dram_tensor("x_in", [D, R], F32, kind="ExternalInput")
    y_out = nc.dram_tensor("y_out", [D, R], F32, kind="ExternalOutput")
    xA = nc.dram_tensor("xA", [D, R], F32, kind="Internal")
    xB = nc.dram_tensor("xB", [D, R], F32, kind="Internal")
    K.stats_hbm = nc.dram_tensor("stats_scr", [2, R], F32, kind="Internal").ap()
    K.xln_hbm = nc.dram_tensor("xln_scr", [D, R], BF16, kind="Internal").ap()
    cc_in = nc.dram_tensor("cc_in", [128, 4160], F32, kind="Internal")
    cc_out = nc.dram_tensor("cc_out", [128, 4160], F32, kind="Internal",
                            addr_space="Shared")
    ag_in = nc.dram_tensor("ag_in", [128, 128], F32, kind="Internal")
    ag_out = nc.dram_tensor("ag_out", [128 * NCORES, 128], F32, kind="Internal",
                            addr_space="Shared")

    attn_names = {"Wq_nat": [D, D], "Wk_nat": [D, D], "WvT": [D, D],
                  "WprojT": [D, D], "b_qkv": [3 * D], "b_proj": [D],
                  "pmT2": [128, 64]}
    wa = {nm: nc.dram_tensor("a_" + nm, sh, F32, kind="ExternalInput")
          for nm, sh in attn_names.items()}
    ffn_w = {nm: nc.dram_tensor("f_" + nm, [D, D], F32, kind="ExternalInput")
             for nm in ("wfT", "wiT", "wcT", "ufT", "uiT", "ucT", "woT", "uoT")}
    ffn_b = {nm: nc.dram_tensor("f_" + nm, [D], F32, kind="ExternalInput")
             for nm in ("bf", "bi", "bc", "bo")}
    f_inc = nc.dram_tensor("f_in_c", [D], F32, kind="ExternalInput")
    omega = nc.dram_tensor("b_omega", [D], F32, kind="ExternalInput")
    BwTd = nc.dram_tensor("b_BwT", [D, D], F32, kind="ExternalInput")
    coresel = nc.dram_tensor("b_coresel", [128, 8], F32, kind="ExternalInput")

    consts = {}
    for nm, shape, dt in (("c_ones_col", [128, 1], F32),
                          ("c_ones_row", [1, 128], F32),
                          ("c_ones_row_bf", [1, 128], BF16),
                          ("c_ident", [128, 128], BF16),
                          ("c_cpool", [128, 16], F32)):
        consts[nm] = nc.dram_tensor(nm, shape, dt, kind="ExternalInput")
    emit_const_loads(K, consts)

    rg = [list(range(NCORES))]
    wdr_a = {nm: wa[nm].ap() for nm in attn_names}
    emit_attn_a(K, x_in.ap(), xA.ap(), wdr_a, cc_in.ap(), cc_out.ap(), rg, "aa")
    K.hard_boundary()
    emit_ffn(K, xA.ap(), xB.ap(), {k: v.ap() for k, v in ffn_w.items()},
             {k: v for k, v in ffn_b.items()}, f_inc, K.xln_hbm, "f1")
    K.hard_boundary()
    wdr_b = {"omega": omega.ap(), "BwT": BwTd.ap(), "coresel": coresel.ap()}
    emit_attn_b(K, xB.ap(), xA.ap(), wdr_b, ag_in.ap(), ag_out.ap(), rg, "bb")
    K.hard_boundary()
    outmarks = emit_ffn(K, xA.ap(), y_out.ap(),
                        {k: v.ap() for k, v in ffn_w.items()},
                        {k: v for k, v in ffn_b.items()}, f_inc, K.xln_hbm, "f2")
    for s_, n_ in outmarks:
        K.p.wait("sync", s_, n_)
    K.retire_all_dma()
    K.p.replay(nc)
    cpool = np.zeros((128, 16), np.float32)
    for i, v in enumerate(K.cpool_vals):
        cpool[:, i] = v
    _NC_CACHE["nc"] = nc
    _NC_CACHE["cpool"] = cpool
    return nc, cpool


def kernel(L0_attn_a_x, L0_attn_a_W_qkv, L0_attn_a_b_qkv, L0_attn_a_W_proj,
           L0_attn_a_b_proj, L0_attn_a_proj_mat, L0_attn_b_omega_raw,
           L0_attn_b_B_weight, L0_ffn_wf, L0_ffn_uf, L0_ffn_bf, L0_ffn_wi,
           L0_ffn_ui, L0_ffn_bi, L0_ffn_wc, L0_ffn_uc, L0_ffn_bc, L0_ffn_wo,
           L0_ffn_uo, L0_ffn_bo, L0_ffn_in_c):
    import ml_dtypes
    from concourse.bass_utils import run_bass_kernel_spmd
    nc, cpool = build_full_nc()

    x = np.ascontiguousarray(L0_attn_a_x, dtype=np.float32)
    Wqkv = np.asarray(L0_attn_a_W_qkv, dtype=np.float32)
    shared = {
        "a_Wq_nat": Wqkv[0:D],
        "a_Wk_nat": Wqkv[D:2 * D],
        "a_WvT": np.ascontiguousarray(Wqkv[2 * D:3 * D].T),
        "a_WprojT": np.ascontiguousarray(np.asarray(L0_attn_a_W_proj, np.float32).T),
        "a_b_qkv": np.asarray(L0_attn_a_b_qkv, np.float32),
        "a_b_proj": np.asarray(L0_attn_a_b_proj, np.float32),
        "a_pmT2": np.ascontiguousarray(
            np.vstack([np.asarray(L0_attn_a_proj_mat, np.float32).T] * 2)),
        "b_omega": np.asarray(L0_attn_b_omega_raw, np.float32),
        "b_BwT": np.ascontiguousarray(np.asarray(L0_attn_b_B_weight, np.float32).T),
        "f_wfT": np.ascontiguousarray(np.asarray(L0_ffn_wf, np.float32).T),
        "f_ufT": np.ascontiguousarray(np.asarray(L0_ffn_uf, np.float32).T),
        "f_wiT": np.ascontiguousarray(np.asarray(L0_ffn_wi, np.float32).T),
        "f_uiT": np.ascontiguousarray(np.asarray(L0_ffn_ui, np.float32).T),
        "f_wcT": np.ascontiguousarray(np.asarray(L0_ffn_wc, np.float32).T),
        "f_ucT": np.ascontiguousarray(np.asarray(L0_ffn_uc, np.float32).T),
        "f_woT": np.ascontiguousarray(np.asarray(L0_ffn_wo, np.float32).T),
        "f_uoT": np.ascontiguousarray(np.asarray(L0_ffn_uo, np.float32).T),
        "f_bf": np.asarray(L0_ffn_bf, np.float32),
        "f_bi": np.asarray(L0_ffn_bi, np.float32),
        "f_bc": np.asarray(L0_ffn_bc, np.float32),
        "f_bo": np.asarray(L0_ffn_bo, np.float32),
        "f_in_c": np.asarray(L0_ffn_in_c, np.float32),
        "c_ones_col": np.ones((128, 1), np.float32),
        "c_ones_row": np.ones((1, 128), np.float32),
        "c_ones_row_bf": np.ones((1, 128), ml_dtypes.bfloat16),
        "c_ident": np.eye(128, dtype=np.float32).astype(ml_dtypes.bfloat16),
        "c_cpool": cpool,
    }
    in_maps = []
    for j in range(NCORES):
        sel = np.zeros((128, 8), np.float32)
        sel[:, j] = 1.0
        xs = x[:, j * NLOC:(j + 1) * NLOC, :]
        xfm = np.ascontiguousarray(
            xs.transpose(2, 0, 1).reshape(D, R))
        im = dict(shared)
        im["x_in"] = xfm
        im["b_coresel"] = sel
        in_maps.append(im)
    res = run_bass_kernel_spmd(nc, in_maps, core_ids=list(range(NCORES)))
    out = np.empty((B, 4096, D), np.float32)
    for j in range(NCORES):
        yj = res.results[j]["y_out"]
        out[:, j * NLOC:(j + 1) * NLOC, :] = (
            yj.reshape(D, B, NLOC).transpose(1, 2, 0))
    return out
